# version 38
# baseline (speedup 1.0000x reference)
import contextlib
import numpy as np
import concourse.bass as bass
import concourse.mybir as mybir
from concourse.bass_utils import run_bass_kernel_spmd

F32 = mybir.dt.float32
BF16 = mybir.dt.bfloat16
I32 = mybir.dt.int32
ACTF = mybir.ActivationFunctionType
ALU = mybir.AluOpType

P = 128
D = 1024
NCH = 8
TPC = 2048
W = 512
NSLAB = 4
HP = 8
SEQ = 4096
C_VELT = float(2 ** 20 + 1)
EPS = 1e-5

ARM_BASE = {0: (0, 512, 3072, 3584), 1: (1024, 1536, 2048, 2560)}


def kb_map(t):
    if t < 8:
        return 0, t
    if t < 24:
        return 1, t - 8
    return 0, t - 16


class Sems:
    def __init__(self, nc, stack):
        self.nc, self.stack, self.d = nc, stack, {}

    def get(self, name):
        if name not in self.d:
            self.d[name] = [self.stack.enter_context(self.nc.semaphore(name)), 0]
        return self.d[name]

    def inc(self, name, instr, k=1):
        e = self.get(name)
        instr.then_inc(e[0], k)
        e[1] += k
        return e[1]

    def dma(self, name, instr):
        return self.inc(name, instr, 16)

    def n(self, name):
        return self.get(name)[1]

    def wait(self, eng, name, val=None):
        e = self.get(name)
        v = e[1] if val is None else val
        if v > 0:
            eng.wait_ge(e[0], v)

    def snapshot(self):
        return {k: v[1] for k, v in self.d.items()}

    def restore(self, snap):
        for k, v in self.d.items():
            v[1] = snap.get(k, 0)


def build_nc():
    nc = bass.Bass("TRN2", target_bir_lowering=False, detect_race_conditions=False)
    stack = contextlib.ExitStack()

    x_in = nc.dram_tensor("x_fm", [D, TPC], F32, kind="ExternalInput").ap()
    wq_in = nc.dram_tensor("wq", [D, D], F32, kind="ExternalInput").ap()
    wk_in = nc.dram_tensor("wk", [D, D], F32, kind="ExternalInput").ap()
    wv_in = nc.dram_tensor("wv", [D, D], F32, kind="ExternalInput").ap()
    wo_in = nc.dram_tensor("wo", [D, D], F32, kind="ExternalInput").ap()
    wf_in = nc.dram_tensor("wf_t", [D, D], F32, kind="ExternalInput").ap()
    bo_in = nc.dram_tensor("bo_rs", [P, NCH], F32, kind="ExternalInput").ap()
    fb_in = nc.dram_tensor("ffn_bias", [1, D], F32, kind="ExternalInput").ap()
    is_in = nc.dram_tensor("in_scale", [1, 1], F32, kind="ExternalInput").ap()
    ws_in = nc.dram_tensor("w_scale", [1, 1], F32, kind="ExternalInput").ap()
    par_in = nc.dram_tensor("parity", [1, 1], I32, kind="ExternalInput").ap()
    y_out = nc.dram_tensor("y_fm", [D, TPC], F32, kind="ExternalOutput").ap()

    k_bounce = nc.dram_tensor("k_bounce", [HP, P, TPC], BF16)
    v_bounce = nc.dram_tensor("v_bounce", [HP, P, 2080], BF16)
    k_gath = nc.dram_tensor("k_gath", [HP, 2, P, TPC], BF16)
    v_gath = nc.dram_tensor("v_gath", [HP, 2, P, 2080], BF16)

    def sbuf(st, name, shape, dt, side="left"):
        return st.enter_context(nc.sbuf_tensor(name, shape, dt, side=side))

    x_st = sbuf(stack, "x_st", [P, NCH * TPC], F32)
    w_a = sbuf(stack, "w_a", [P, NCH * D], BF16)
    tri = sbuf(stack, "tri", [P, P], BF16)
    e_sel = sbuf(stack, "e_sel", [16, NCH * P], BF16)
    ones128 = sbuf(stack, "ones128", [1, P], F32)
    inv1024 = sbuf(stack, "inv1024", [P, 1], BF16)
    inv1024f = sbuf(stack, "inv1024f", [P, 1], F32)
    onescol = sbuf(stack, "onescol", [P, 1], BF16)
    onesw = sbuf(stack, "onesw", [1, D], BF16)
    fbrow = sbuf(stack, "fbrow", [1, D], BF16)
    epsb = sbuf(stack, "epsb", [1, 1], F32)
    bo_sb = sbuf(stack, "bo_sb", [P, NCH], F32)
    sc_sb = sbuf(stack, "sc_sb", [1, 4], F32)
    rinsc = sbuf(stack, "rinsc", [P, 1], F32)
    ah_bc = sbuf(stack, "ah_bc", [P, 1], F32)
    par_sb = sbuf(stack, "par_sb", [1, 1], I32)
    wtmp = sbuf(stack, "wtmp", [P, 3 * W], F32)

    stR = contextlib.ExitStack()
    q_fm = sbuf(stR, "q_fm", [P, NCH * TPC], BF16, side="right")

    stL1 = contextlib.ExitStack()
    wf_sb = sbuf(stL1, "wf_sb", [P, NCH * D], BF16)
    u_bf = sbuf(stL1, "u_bf", [P, NCH * TPC], BF16)
    shad = sbuf(stL1, "shad", [P, NCH * W], BF16)
    lnf_a = sbuf(stL1, "lnf_a", [P, 2 * W], F32)
    lnf_b = sbuf(stL1, "lnf_b", [P, 2 * W], F32)
    strip_a = sbuf(stL1, "strip_a", [1, W], F32)
    strip_b = sbuf(stL1, "strip_b", [1, W], F32)
    k_stage = sbuf(stL1, "k_stage", [P, TPC], BF16)
    v_stage = sbuf(stL1, "v_stage", [P, 2080], BF16)
    iot = sbuf(stL1, "iot", [P, P], F32)
    fb_sb = sbuf(stL1, "fb_sb", [1, D], F32)

    S = Sems(nc, stack)
    TE, VE, SE, GE, SY = nc.tensor, nc.vector, nc.scalar, nc.gpsimd, nc.sync

    x_slab_loaded = []
    for sl in range(NSLAB):
        for c in range(NCH):
            S.dma("s_in", SY.dma_start(
                out=x_st[:, c * TPC + sl * W:c * TPC + (sl + 1) * W],
                in_=x_in[c * P:(c + 1) * P, sl * W:(sl + 1) * W]))
        x_slab_loaded.append(S.n("s_in"))
    x_loaded = S.n("s_in")
    S.dma("s_in", SY.dma_start(out=par_sb[:, :], in_=par_in))
    S.dma("s_in", SY.dma_start(out=bo_sb[:, :], in_=bo_in))
    S.dma("s_in", SY.dma_start(out=fb_sb[:, :], in_=fb_in))
    S.dma("s_in", SY.dma_start(out=sc_sb[:, 0:1], in_=is_in))
    S.dma("s_in", SY.dma_start(out=sc_sb[:, 1:2], in_=ws_in))
    smalls_loaded = S.n("s_in")

    def load_weight(src, dst):
        lanes = (SY, GE, SE)
        for piece in range(16):
            cc, half = piece // 2, piece % 2
            wb = wtmp[:, (piece % 3) * W:(piece % 3 + 1) * W]
            lane = lanes[piece % 3]
            if piece >= 3:
                S.wait(lane, "s_wc", S.n("s_wc") - 2)
            else:
                S.wait(lane, "s_wc")
            S.dma("s_w%d" % (piece % 3), lane.dma_start(
                out=wb, in_=src[cc * P:(cc + 1) * P, half * W:(half + 1) * W]))
            VE.wait_ge(S.get("s_w%d" % (piece % 3))[0], S.n("s_w%d" % (piece % 3)))
            S.inc("s_wc", VE.tensor_copy(dst[:, cc * D + half * W:cc * D + (half + 1) * W],
                                         wb))
        return S.n("s_wc")

    wf_ready = load_weight(wf_in, wf_sb)

    S.inc("s_set", GE.memset(ones128[:, :], 1.0))
    S.inc("s_set", GE.memset(inv1024[:, :], 1.0 / 1024.0))
    S.inc("s_set", GE.memset(inv1024f[:, :], 1.0 / 1024.0))
    S.inc("s_set", GE.memset(onescol[:, :], 1.0))
    S.inc("s_set", GE.memset(onesw[:, :], 1.0))
    S.inc("s_set", GE.memset(
        v_stage[:, :].rearrange("p (a b) -> p a b", b=65)[:, :, 64:65], 1.0))
    S.inc("s_set", GE.memset(epsb[:, :], EPS))
    S.inc("s_set", GE.iota(iot[:, :], pattern=[[1, P]], base=0, channel_multiplier=-1,
                           allow_small_or_imprecise_dtypes=True))
    S.inc("s_set", GE.iota(lnf_a[0:16, 0:NCH * P].rearrange("p (h f m) -> p h f m", h=NCH, f=2),
                           pattern=[[-2, NCH], [-1, 2], [0, 64]], base=0, channel_multiplier=1,
                           allow_small_or_imprecise_dtypes=True))
    S.wait(VE, "s_set")
    S.inc("s_set", VE.tensor_single_scalar(tri[:, :], iot[:, :], 0.0, ALU.is_ge))
    S.inc("s_set", VE.tensor_single_scalar(e_sel[:, :], lnf_a[0:16, 0:NCH * P], 0.0,
                                           ALU.is_equal))
    setup_done = S.n("s_set")

    stA = contextlib.ExitStack()
    psA = lambda name, shape: stA.enter_context(nc.psum_tensor(name, shape, F32))
    ps_stats = psA("ps_stats", [P, 1024])
    ps_rb = psA("ps_rb", [P, W])
    ps_bb = psA("ps_bb", [P, W])
    ps_pj = [psA("ps_pj0", [P, 1024]), psA("ps_pj1", [P, 1024])]

    sA, sB = strip_a[0:1, 0:W], strip_b[0:1, 0:W]
    sC, sD = lnf_a[0:1, 0:W], lnf_a[0:1, W:2 * W]
    sE, sF = lnf_b[0:1, 0:W], lnf_b[0:1, W:2 * W]
    S.inc("s_set", GE.memset(strip_a[:, :], 1.0))
    S.wait(VE, "s_set")
    VE.wait_ge(S.get("s_in")[0], smalls_loaded)
    VE.tensor_scalar_mul(sB, sA, sc_sb[0:1, 0:1])
    VE.tensor_scalar_mul(sC, sA, sc_sb[0:1, 1:2])
    VE.tensor_mul(sD, sB, sC)
    VE.tensor_scalar_mul(sE, sD, 0.5)
    VE.tensor_single_scalar(sF, sD, 0.0, ALU.is_ge)
    VE.tensor_scalar(sF, sF, 2.0, -1.0, ALU.mult, ALU.add)
    S.inc("s_rc", VE.scalar_tensor_tensor(sA, sD, -1.0, sD, ALU.mult, ALU.max))
    S.wait(SE, "s_rc")
    SE.activation(sA, sA, ACTF.Ln, scale=1.0)
    S.inc("s_rc", SE.activation(sA, sA, ACTF.Exp, scale=-1.0))
    S.wait(VE, "s_rc")
    VE.tensor_mul(sA, sA, sF)
    S.inc("s_set", VE.tensor_scalar_mul(fbrow[:, :], fb_sb[:, :], sA[0:1, 0:1]))
    VE.tensor_single_scalar(sC, sB, 0.0, ALU.is_ge)
    VE.tensor_scalar(sC, sC, 2.0, -1.0, ALU.mult, ALU.add)
    S.inc("s_rc", VE.scalar_tensor_tensor(sB, sB, -1.0, sB, ALU.mult, ALU.max))
    S.wait(SE, "s_rc")
    SE.activation(sB, sB, ACTF.Ln, scale=1.0)
    S.inc("s_rc", SE.activation(sB, sB, ACTF.Exp, scale=-1.0))
    S.wait(VE, "s_rc")
    S.inc("s_rc", VE.tensor_mul(sB, sB, sC))
    S.wait(TE, "s_rc")
    TE.wait_ge(S.get("s_set")[0], S.n("s_set") - 1)
    TE.matmul(ps_stats[:, 2:3], ones128[0:1, :], sE[0:1, 0:1], start=True, stop=True)
    S.inc("s_mm0", TE.matmul(ps_stats[:, 3:4], ones128[0:1, :], sB[0:1, 0:1],
                             start=True, stop=True))
    S.wait(VE, "s_mm0")
    VE.tensor_copy(ah_bc[:, :], ps_stats[:, 2:3])
    S.inc("s_set", VE.tensor_copy(rinsc[:, :], ps_stats[:, 3:4]))
    setup_done = S.n("s_set")
    ah_ready = S.n("s_set")

    def emit_ln(src_wait, quant, pre, out_bf, slabs=None):
        done = []
        for s in (range(NSLAB) if slabs is None else slabs):
            col = s * W
            if s == 0:
                S.wait(TE, "s_set", setup_done)
                S.wait(SE, "s_set", setup_done)
                S.wait(VE, "s_set", setup_done)
            for c in range(NCH):
                src_wait(VE, s, c)
                S.inc(pre + "_cp", VE.tensor_copy(shad[:, c * W:(c + 1) * W],
                                                  x_st[:, c * TPC + col:c * TPC + col + W]))
            base_cp = S.n(pre + "_cp") - NCH
            base_m1 = S.n(pre + "_m1")
            for c in range(NCH):
                TE.wait_ge(S.get(pre + "_cp")[0], base_cp + c + 1)
                if S.n(pre + "_sr") > 0 and c == 0:
                    S.wait(TE, pre + "_sr")
                S.inc(pre + "_m1", TE.matmul(ps_stats[0:1, 0:W], inv1024[:, 0:1],
                                             shad[:, c * W:(c + 1) * W],
                                             start=(c == 0), stop=(c == NCH - 1)))
            for c in range(NCH):
                VE.wait_ge(S.get(pre + "_m1")[0], base_m1 + c + 1)
                S.inc(pre + "_sq", VE.tensor_mul(shad[:, c * W:(c + 1) * W],
                                                 shad[:, c * W:(c + 1) * W],
                                                 shad[:, c * W:(c + 1) * W]))
            base_sq = S.n(pre + "_sq") - NCH
            for c in range(NCH):
                TE.wait_ge(S.get(pre + "_sq")[0], base_sq + c + 1)
                S.inc(pre + "_m2", TE.matmul(ps_stats[0:1, W:2 * W], inv1024[:, 0:1],
                                             shad[:, c * W:(c + 1) * W],
                                             start=(c == 0), stop=(c == NCH - 1)))
            S.wait(VE, pre + "_m2")
            VE.tensor_copy(strip_b[:, :], ps_stats[0:1, 0:W])
            VE.tensor_mul(strip_a[:, :], strip_b[:, :], strip_b[:, :])
            S.inc(pre + "_va", VE.tensor_sub(strip_a[:, :], ps_stats[0:1, W:2 * W],
                                             strip_a[:, :]))
            S.wait(SE, pre + "_va")
            SE.activation(strip_b[:, :], strip_a[:, :], ACTF.Ln, bias=epsb[0:1, 0:1], scale=1.0)
            S.inc(pre + "_rs", SE.activation(strip_b[:, :], strip_b[:, :], ACTF.Exp, scale=-0.5))
            S.wait(VE, pre + "_rs")
            if quant:
                VE.wait_ge(S.get("s_set")[0], setup_done)
                VE.tensor_scalar_mul(strip_b[:, :], strip_b[:, :], rinsc[0:1, 0:1])
            S.inc(pre + "_sr", VE.scalar_tensor_tensor(strip_a[:, :], ps_stats[0:1, 0:W], -1.0,
                                                       strip_b[:, :], ALU.mult, ALU.mult))
            S.wait(TE, pre + "_sr")
            if S.n(pre + "_u") > 0:
                S.wait(TE, pre + "_u")
            TE.matmul(ps_rb[:, :], ones128[0:1, :], strip_b[0:1, :], start=True, stop=True)
            S.inc(pre + "_bc", TE.matmul(ps_bb[:, :], ones128[0:1, :], strip_a[0:1, :],
                                         start=True, stop=True))
            S.wait(VE, pre + "_bc")
            for c in range(NCH):
                a = lnf_a[:, (c % 2) * W:(c % 2 + 1) * W]
                b = lnf_b[:, (c % 2) * W:(c % 2 + 1) * W]
                xs = x_st[:, c * TPC + col:c * TPC + col + W]
                ub = out_bf[:, c * TPC + col:c * TPC + col + W]
                if quant:
                    if c >= 2:
                        VE.wait_ge(S.get(pre + "_u")[0], S.n(pre + "_u") - 1)
                    VE.tensor_mul(a, xs, ps_rb[:, :])
                    VE.tensor_add(a, a, ps_bb[:, :])
                    S.inc(pre + "_cl", VE.tensor_scalar(a, a, -448.0, 448.0, ALU.max, ALU.min))
                    S.wait(SE, pre + "_cl")
                    S.inc(pre + "_t", SE.activation(b, a, ACTF.Copy, scale=C_VELT))
                    S.wait(VE, pre + "_t")
                    VE.tensor_sub(a, b, a)
                    S.inc(pre + "_u", VE.tensor_sub(ub, b, a))
                else:
                    VE.tensor_mul(a, xs, ps_rb[:, :])
                    S.inc(pre + "_u", VE.tensor_add(ub, a, ps_bb[:, :]))
            done.append(S.n(pre + "_u"))
        return done

    def emit_proj(w_sb, rhs_sb, consume, pre, wait_rhs, wait_w, bias_row=None,
                  pr_outer=False, units=None, ucols=1024, chunk_wait=None):
        if units is None:
            units = ([(j, pr * 1024) for pr in range(2) for j in range(NCH)] if pr_outer
                     else [(j, pr * 1024) for j in range(NCH) for pr in range(2)])
        for ui, (j, coff) in enumerate(units):
            u_idx = S.n(pre + "_mm")
            pb = ps_pj[u_idx % 2]
            if u_idx >= 2:
                S.wait(TE, pre + "_ev", S.n(pre + "_ev") - 1)
            if chunk_wait is None or ui > 0:
                wait_rhs(TE, coff)
            if u_idx == 0:
                wait_w(TE)
            last = None
            for cs in range(0, ucols, W):
                for i in range(NCH):
                    if chunk_wait is not None and ui == 0:
                        chunk_wait(TE, coff, i, cs)
                    last = TE.matmul(pb[:, cs:cs + W],
                                     w_sb[:, i * D + j * P:i * D + (j + 1) * P],
                                     rhs_sb[:, i * TPC + coff + cs:
                                            i * TPC + coff + cs + W],
                                     start=(i == 0),
                                     stop=(bias_row is None and i == NCH - 1))
            if bias_row is not None:
                for cs in range(0, ucols, W):
                    last = TE.matmul(pb[:, cs:cs + W], bias_row[0:1, j * P:(j + 1) * P],
                                     onesw[0:1, cs:cs + W], start=False, stop=True)
            S.inc(pre + "_mm", last)
            consume(j, coff, pb)

    def src_wait_x0(eng, s, c):
        eng.wait_ge(S.get("s_in")[0], 16 * (8 * s + c + 1))

    def ffn_consume(pre, ucols=1024):
        def consume(j, coff, pb):
            S.wait(VE, pre + "_mm")
            VE.wait_ge(S.get("s_set")[0], ah_ready)
            xs = x_st[:, j * TPC + coff:j * TPC + coff + ucols]
            S.inc(pre + "_ev", VE.scalar_tensor_tensor(xs, pb[:, 0:ucols], ah_bc[:, 0:1],
                                                       xs, ALU.mult, ALU.add))
        return consume

    ln1_done = []

    def wait_u1(eng, coff):
        eng.wait_ge(S.get("l1_u")[0], ln1_done[coff // W])

    def wait_wf(eng):
        eng.wait_ge(S.get("s_wc")[0], wf_ready)
        eng.wait_ge(S.get("s_set")[0], setup_done)

    def chase_l1(eng, coff, i, cs):
        eng.wait_ge(S.get("l1_u")[0], 8 * (coff // W) + i + 1)

    f1c = ffn_consume("f1", W)
    for s in range(NSLAB):
        ln1_done += emit_ln(src_wait_x0, True, "l1", u_bf, slabs=[s])
        emit_proj(wf_sb, u_bf, f1c, "f1", wait_u1, wait_wf, bias_row=fbrow,
                  units=[(j, s * W) for j in range(NCH)], ucols=W,
                  chunk_wait=chase_l1)
    f1_ev_done = S.n("f1_ev")

    wk_ready = load_weight(wk_in, w_a)

    f1_mm_done = S.n("f1_mm")

    def src_wait_x1(eng, s, c):
        eng.wait_ge(S.get("f1_ev")[0], min(8 * s + c + 1, f1_ev_done))
        eng.wait_ge(S.get("f1_mm")[0], min(8 * s + c + 1, f1_mm_done))

    ln2_done = []

    def wait_xln2(eng, coff):
        eng.wait_ge(S.get("l2_u")[0], ln2_done[2 * (coff // 1024) + 1])

    def k_consume(j, coff, pb):
        pr = coff // 1024
        slot = j % 2
        u = S.n("k_ev")
        S.wait(VE, "k_mm")
        if u >= 2:
            VE.wait_ge(S.get("kv_od")[0], 16 * (u - 1))
        S.inc("k_ev", VE.tensor_copy(k_stage[:, slot * 1024:(slot + 1) * 1024],
                                     pb[:, :]))
        S.wait(GE, "k_ev")
        S.dma("kv_od", GE.dma_start(out=k_bounce[j, :, pr * 1024:(pr + 1) * 1024],
                                    in_=k_stage[:, slot * 1024:(slot + 1) * 1024]))
        if pr == 1:
            GE.wait_ge(S.get("kv_od")[0], 16 * S.n("k_ev"))
            S.inc("s_ck", GE.collective_compute(
                "AllGather", ALU.bypass,
                replica_groups=[[0, 1], [2, 3], [4, 5], [6, 7]],
                ins=[k_bounce[j, :, :].opt()],
                outs=[k_gath[j, :, :, :].opt()],
            ))

    def wait_wk(eng):
        eng.wait_ge(S.get("s_wc")[0], wk_ready)
        S.wait(eng, "f1_ev")

    def chase_l2(eng, coff, i, cs):
        eng.wait_ge(S.get("l2_u")[0], 8 * (2 * (coff // 1024) + cs // W) + i + 1)

    for half in range(2):
        for s in (2 * half, 2 * half + 1):
            ln2_done += emit_ln(src_wait_x1, False, "l2", u_bf, slabs=[s])
        emit_proj(w_a, u_bf, k_consume, "k", wait_xln2, wait_wk,
                  units=[(j, half * 1024) for j in range(NCH)], chunk_wait=chase_l2)

    SY.wait_ge(S.get("f1_mm")[0], f1_mm_done)
    wv_ready = load_weight(wv_in, wf_sb)

    v_stage_view = v_stage[:, :].rearrange("p (hp t h c) -> p hp t h c", hp=HP, t=2, h=2)
    for e8 in range(8):
        for tt in range(2):
            t = e8 * 2 + tt
            s = t // 4
            pb = ps_pj[t % 2]
            if t >= 2:
                S.wait(TE, "v_ev", S.n("v_ev") - 1)
            TE.wait_ge(S.get("l2_u")[0], ln2_done[s])
            if t == 0:
                TE.wait_ge(S.get("s_wc")[0], wv_ready)
                S.wait(TE, "k_ev")
            last = None
            for i in range(NCH):
                for h2 in range(2):
                    cs = h2 * W
                    last = TE.matmul(pb[:, cs:cs + W],
                                     u_bf[:, i * TPC + t * P:i * TPC + (t + 1) * P],
                                     wf_sb[:, i * D + cs:i * D + cs + W],
                                     start=(i == 0), stop=(i == NCH - 1))
            S.inc("v_mm", last)
            S.wait(VE, "v_mm")
            if e8 >= 1 and tt == 0:
                VE.wait_ge(S.get("kv_vd")[0], 16 * (8 * (e8 - 1) + 8))
            S.inc("v_ev", VE.tensor_copy(
                v_stage_view[:, :, tt, :, 0:64],
                pb[:, :].rearrange("p (hp h c) -> p hp h c", hp=HP, h=2)))
        S.wait(GE, "v_ev")
        for hp in range(HP):
            S.dma("kv_vd", GE.dma_start(
                out=v_bounce[hp, :, e8 * 260:(e8 + 1) * 260],
                in_=v_stage[:, hp * 260:(hp + 1) * 260]))

    SY.wait_ge(S.get("k_mm")[0], S.n("k_mm"))
    wq_ready = load_weight(wq_in, w_a)

    S.wait(GE, "kv_vd")
    for hp in range(HP):
        S.inc("s_cv", GE.collective_compute(
            "AllGather", ALU.bypass,
            replica_groups=[[0, 1], [2, 3], [4, 5], [6, 7]],
            ins=[v_bounce[hp, :, :].opt()],
            outs=[v_gath[hp, :, :, :].opt()],
        ))

    def q_consume(j, coff, pb):
        S.wait(VE, "q_mm")
        S.inc("q_ev", VE.tensor_copy(q_fm[:, j * TPC + coff:j * TPC + coff + 1024],
                                     pb[:, :]))

    def wait_wq(eng):
        eng.wait_ge(S.get("s_wc")[0], wq_ready)
        S.wait(eng, "v_ev")

    emit_proj(w_a, u_bf, q_consume, "q", wait_xln2, wait_wq)

    SY.wait_ge(S.get("q_mm")[0], S.n("q_mm"))
    wo_ready = load_weight(wo_in, w_a)

    stA.close()
    stL1.close()

    stL2a = contextlib.ExitStack()
    ctx_sb = sbuf(stL2a, "ctx_sb", [P, HP * TPC], BF16)
    stL2b = contextlib.ExitStack()
    sums_sb = sbuf(stL2b, "sums_sb", [16, TPC], BF16)
    sstrip = sbuf(stL2b, "sstrip", [1, 2 * W], BF16)
    kv_sb = sbuf(stL2b, "kv_sb", [P, 2 * 8256], BF16)
    pt_sb = sbuf(stL2b, "pt_sb", [P, 2 * 3072], BF16)

    for eng in (TE, VE, SE, GE, SY):
        eng.wait_ge(S.get("s_in")[0], smalls_loaded)
    parv = nc.values_load(par_sb[0:1, 0:1], min_val=0, max_val=1,
                          skip_runtime_bounds_check=True)

    stB = contextlib.ExitStack()
    psB = lambda name, shape: stB.enter_context(nc.psum_tensor(name, shape, F32))
    sc_ps = psB("sc_ps", [P, 3072])
    ctx_psA = psB("ctx_psA", [P, W])
    ctx_psB = psB("ctx_psB", [P, W])
    S.inc("ps_i", VE.memset(sc_ps[:, :], 0.0))

    def emit_arm(arm):
        base = ARM_BASE[arm]
        S.wait(TE, "ps_i")
        av_after_hp = {}
        ng = [0]
        pend = []
        LAG = 2

        def do_scores(hp, s, h, gkbs):
            qcol = hp * TPC + s * W
            TE.wait_ge(S.get("a_kv")[0], k_ready_of[hp])
            if ng[0] >= 2:
                S.wait(TE, "a_exp", ng[0] - 1)
            sb_ = (ng[0] % 2) * 1536
            K_all = K_all_of[hp]
            last = None
            for si, kb in enumerate(gkbs):
                off = max(0, P * kb - base[s])
                slot, tile = kb_map(kb)
                kcol = slot * TPC + P * tile
                last = TE.matmul(
                    sc_ps[:, sb_ + si * W + off:sb_ + (si + 1) * W],
                    K_all[h * 64:h * 64 + 64, kcol:kcol + P],
                    q_fm[h * 64:h * 64 + 64, qcol + off:qcol + W],
                    start=True, stop=True, tile_position=(h * 64, 0))
            S.inc("a_sc", last)

        def do_exp(ngk):
            S.wait(SE, "a_sc")
            if ng[0] >= 4:
                S.wait(SE, "a_av", ng[0] - 3)
            sb_ = (ng[0] % 2) * 1536
            pb_ = (ng[0] % 4) * 1536
            wexp = 512 * ngk
            S.inc("a_exp", SE.activation(pt_sb[:, pb_:pb_ + wexp],
                                         sc_ps[:, sb_:sb_ + wexp],
                                         ACTF.Exp, scale=0.125))

        def do_masks(s, h, gkbs, my_ng, n_kb):
            diag = [(si, kb) for si, kb in enumerate(gkbs) if kb >= n_kb - 4]
            if not diag:
                return S.n("a_msk")
            VE.wait_ge(S.get("a_exp")[0], my_ng + 1)
            pb_ = (my_ng % 4) * 1536
            last = None
            for si, kb in diag:
                off = max(0, P * kb - base[s])
                pc = pb_ + si * W + off
                last = VE.tensor_mul(pt_sb[:, pc:pc + P], pt_sb[:, pc:pc + P],
                                     tri[:, :])
            S.inc("a_msk", last)
            return S.n("a_msk")

        def do_av(hp, s, h, gkbs, my_ng, msk_cnt):
            TE.wait_ge(S.get("a_exp")[0], my_ng + 1)
            TE.wait_ge(S.get("a_kv")[0], kv_full_of[hp])
            if msk_cnt > 0:
                S.wait(TE, "a_msk", msk_cnt)
            if gkbs[0] == 0 and S.n("a_cx") > 0:
                S.wait(TE, "a_cx")
            pb_ = (my_ng % 4) * 1536
            ctx = ctx_psA if h == 0 else ctx_psB
            last = None
            for si, kb in enumerate(gkbs):
                off = max(0, P * kb - base[s])
                slot, tile = kb_map(kb)
                vcol = slot * 2080 + 130 * tile
                pc = pb_ + si * W + off
                last = TE.matmul(ctx[0:65, off:W],
                                 V_all_of[hp][:, vcol + h * 65:vcol + h * 65 + 65],
                                 pt_sb[:, pc:pc + (W - off)],
                                 start=(kb == 0), stop=False,
                                 tile_position=(0, 0), skip_group_check=True)
            S.inc("a_av", last)

        def do_evac(hp, s):
            S.wait(VE, "a_av")
            S.wait(VE, "a_sm")
            ccol = hp * TPC + s * W
            VE.tensor_copy(ctx_sb[0:64, ccol:ccol + W], ctx_psA[0:64, :])
            VE.tensor_copy(ctx_sb[64:128, ccol:ccol + W], ctx_psB[0:64, :])
            VE.tensor_copy(sstrip[0:1, 0:W], ctx_psA[64:65, :])
            S.inc("a_cx", VE.tensor_copy(sstrip[0:1, W:2 * W], ctx_psB[64:65, :]))
            S.wait(SY, "a_cx")
            S.dma("a_sm", SY.dma_start(out=sums_sb[2 * hp:2 * hp + 1, s * W:(s + 1) * W],
                                       in_=sstrip[0:1, 0:W]))
            S.dma("a_sm", SY.dma_start(out=sums_sb[2 * hp + 1:2 * hp + 2, s * W:(s + 1) * W],
                                       in_=sstrip[0:1, W:2 * W]))

        def drain_one():
            hp_, s_, h_, gkbs_, my_, mc_, lastslab = pend.pop(0)
            do_av(hp_, s_, h_, gkbs_, my_, mc_)
            if lastslab:
                do_evac(hp_, s_)
                if s_ == NSLAB - 1:
                    av_after_hp[hp_] = S.n("a_av")

        k_ready_of, kv_full_of, K_all_of, V_all_of = {}, {}, {}, {}

        def load_hp(hp):
            buf = hp % 2
            kvb = buf * 8256
            S.wait(GE, "s_ck", hp + 1)
            if hp >= 2:
                GE.wait_ge(S.get("a_av")[0], av_after_hp[hp - 2])
            for slot in range(2):
                S.dma("a_kv", GE.dma_start(
                    out=kv_sb[:, kvb + slot * TPC:kvb + (slot + 1) * TPC],
                    in_=k_gath[hp, slot, :, :]))
            k_ready_of[hp] = S.n("a_kv")
            S.wait(GE, "s_cv", hp + 1)
            for slot in range(2):
                S.dma("a_kv", GE.dma_start(
                    out=kv_sb[:, kvb + 4096 + slot * 2080:kvb + 4096 + (slot + 1) * 2080],
                    in_=v_gath[hp, slot, :, :]))
            kv_full_of[hp] = S.n("a_kv")
            K_all_of[hp] = kv_sb[:, kvb:kvb + 4096]
            V_all_of[hp] = kv_sb[:, kvb + 4096:kvb + 8256]

        cur_hp = -1
        for hp in range(HP):
            for s in range(NSLAB):
                n_kb = (base[s] + W) // P
                G = [list(range(i, min(i + 3, n_kb))) for i in range(0, n_kb, 3)]
                seq = [(h, gkbs) for gkbs in G for h in (0, 1)]
                for li, (h, gkbs) in enumerate(seq):
                    if hp != cur_hp:
                        load_hp(hp)
                        cur_hp = hp
                    my = ng[0]
                    do_scores(hp, s, h, gkbs)
                    do_exp(len(gkbs))
                    mc = do_masks(s, h, gkbs, my, n_kb)
                    pend.append((hp, s, h, gkbs, my, mc, li == len(seq) - 1))
                    ng[0] += 1
                    if li % 2 == 1:
                        while len(pend) > LAG:
                            drain_one()
        while pend:
            drain_one()
        S.wait(VE, "a_cx")
        S.inc("a_done", VE.engine_nop())

    snap = S.snapshot()
    with nc.If(parv):
        emit_arm(1)
    nc.end_ifs()
    arm1 = S.n("a_done")
    S.restore(snap)
    with nc.If(1 - parv):
        emit_arm(0)
    nc.end_ifs()
    assert S.n("a_done") == arm1 == snap.get("a_done", 0) + 1

    stB.close()

    stC = contextlib.ExitStack()
    ps_nrm = stC.enter_context(nc.psum_tensor("ps_nrm", [P, TPC], F32))
    for eng in (GE, SE, VE, TE):
        S.wait(eng, "a_done")
    S.wait(SE, "a_sm")
    SE.activation(sums_sb[:, :], sums_sb[:, :], ACTF.Ln, scale=1.0)
    S.inc("n_rc", SE.activation(sums_sb[:, :], sums_sb[:, :], ACTF.Exp, scale=-1.0))
    nrm_half = []
    for step in range(2 * HP):
        pr, hp = step // HP, step % HP
        nb = (step % 2) * 1024
        S.wait(TE, "n_rc")
        if step >= 2:
            S.wait(TE, "n_ev", S.n("n_ev") - 1)
        last = None
        for h4 in range(2):
            cs = h4 * W
            last = TE.matmul(ps_nrm[:, nb + cs:nb + cs + W],
                             e_sel[:, hp * P:(hp + 1) * P],
                             sums_sb[0:16, pr * 1024 + cs:pr * 1024 + cs + W],
                             start=True, stop=True)
        S.inc("n_mm", last)
        S.wait(VE, "n_mm")
        ccol = hp * TPC + pr * 1024
        S.inc("n_ev", VE.tensor_mul(ctx_sb[:, ccol:ccol + 1024],
                                    ctx_sb[:, ccol:ccol + 1024],
                                    ps_nrm[:, nb:nb + 1024]))
        if hp == HP - 1:
            nrm_half.append(S.n("n_ev"))
    stC.close()
    nrm_done = S.n("n_ev")

    stR.close()
    stL2b.close()

    stL3 = contextlib.ExitStack()
    u2_bf = sbuf(stL3, "u2_bf", [P, NCH * TPC], BF16)
    wf2_sb = sbuf(stL3, "wf2_sb", [P, NCH * D], BF16)
    shad2 = sbuf(stL3, "shad2", [P, NCH * W], BF16)
    lnf2_a = sbuf(stL3, "lnf2_a", [P, 2 * W], F32)
    lnf2_b = sbuf(stL3, "lnf2_b", [P, 2 * W], F32)
    strip2_a = sbuf(stL3, "strip2_a", [1, W], F32)
    strip2_b = sbuf(stL3, "strip2_b", [1, W], F32)

    stD = contextlib.ExitStack()
    psD = lambda name, shape: stD.enter_context(nc.psum_tensor(name, shape, F32))
    ps_stats2 = psD("ps_stats2", [P, 1024])
    ps_rb2 = psD("ps_rb2", [P, W])
    ps_bb2 = psD("ps_bb2", [P, W])
    ps_pj2 = [psD("ps_pj0b", [P, 1024]), psD("ps_pj1b", [P, 1024])]

    def emit_ln4(src_wait, pre, out_bf, slabs=None):
        done = []
        for s in (range(NSLAB) if slabs is None else slabs):
            col = s * W
            if s == 0:
                S.wait(TE, "s_set", setup_done)
                S.wait(SE, "s_set", setup_done)
                S.wait(VE, "s_set", setup_done)
            base_m1 = S.n(pre + "_m1")
            for c in range(NCH):
                src_wait(TE, s, c)
                if S.n(pre + "_sr") > 0 and c == 0:
                    S.wait(TE, pre + "_sr")
                S.inc(pre + "_m1", TE.matmul(ps_stats2[0:1, 0:W], inv1024f[:, 0:1],
                                             x_st[:, c * TPC + col:c * TPC + col + W],
                                             start=(c == 0), stop=(c == NCH - 1)))
            for c in range(NCH):
                src_wait(VE, s, c)
                S.inc(pre + "_sq", VE.tensor_mul(shad2[:, c * W:(c + 1) * W],
                                                 x_st[:, c * TPC + col:c * TPC + col + W],
                                                 x_st[:, c * TPC + col:c * TPC + col + W]))
            base_sq = S.n(pre + "_sq") - NCH
            for c in range(NCH):
                TE.wait_ge(S.get(pre + "_sq")[0], base_sq + c + 1)
                S.inc(pre + "_m2", TE.matmul(ps_stats2[0:1, W:2 * W], inv1024[:, 0:1],
                                             shad2[:, c * W:(c + 1) * W],
                                             start=(c == 0), stop=(c == NCH - 1)))
            S.wait(VE, pre + "_m2")
            VE.tensor_copy(strip2_b[:, :], ps_stats2[0:1, 0:W])
            VE.tensor_mul(strip2_a[:, :], strip2_b[:, :], strip2_b[:, :])
            S.inc(pre + "_va", VE.tensor_sub(strip2_a[:, :], ps_stats2[0:1, W:2 * W],
                                             strip2_a[:, :]))
            S.wait(SE, pre + "_va")
            SE.activation(strip2_b[:, :], strip2_a[:, :], ACTF.Ln, bias=epsb[0:1, 0:1], scale=1.0)
            S.inc(pre + "_rs", SE.activation(strip2_b[:, :], strip2_b[:, :], ACTF.Exp, scale=-0.5))
            S.wait(VE, pre + "_rs")
            VE.wait_ge(S.get("s_set")[0], setup_done)
            VE.tensor_scalar_mul(strip2_b[:, :], strip2_b[:, :], rinsc[0:1, 0:1])
            S.inc(pre + "_sr", VE.scalar_tensor_tensor(strip2_a[:, :], ps_stats2[0:1, 0:W], -1.0,
                                                       strip2_b[:, :], ALU.mult, ALU.mult))
            S.wait(TE, pre + "_sr")
            if S.n(pre + "_u") > 0:
                S.wait(TE, pre + "_u")
            TE.matmul(ps_rb2[:, :], ones128[0:1, :], strip2_b[0:1, :], start=True, stop=True)
            S.inc(pre + "_bc", TE.matmul(ps_bb2[:, :], ones128[0:1, :], strip2_a[0:1, :],
                                         start=True, stop=True))
            S.wait(VE, pre + "_bc")
            for c in range(NCH):
                a = lnf2_a[:, (c % 2) * W:(c % 2 + 1) * W]
                b = lnf2_b[:, (c % 2) * W:(c % 2 + 1) * W]
                xs = x_st[:, c * TPC + col:c * TPC + col + W]
                ub = out_bf[:, c * TPC + col:c * TPC + col + W]
                if c >= 2:
                    VE.wait_ge(S.get(pre + "_u")[0], S.n(pre + "_u") - 1)
                VE.tensor_mul(a, xs, ps_rb2[:, :])
                VE.tensor_add(a, a, ps_bb2[:, :])
                S.inc(pre + "_cl", VE.tensor_scalar(a, a, -448.0, 448.0, ALU.max, ALU.min))
                S.wait(SE, pre + "_cl")
                S.inc(pre + "_t", SE.activation(b, a, ACTF.Copy, scale=C_VELT))
                S.wait(VE, pre + "_t")
                VE.tensor_sub(a, b, a)
                S.inc(pre + "_u", VE.tensor_sub(ub, b, a))
            done.append(S.n(pre + "_u"))
        return done

    def emit_proj4(w_sb, rhs_sb, consume, pre, wait_rhs, wait_w, bias_row=None,
                   pr_outer=False, chunk_wait=None, prs=None):
        order = ([(j, pr) for pr in (range(2) if prs is None else prs)
                  for j in range(NCH)] if (pr_outer or prs is not None)
                 else [(j, pr) for j in range(NCH) for pr in range(2)])
        for j, pr in order:
            if True:
                u_idx = S.n(pre + "_mm")
                pb = ps_pj2[u_idx % 2]
                first = chunk_wait is not None and j == 0
                if u_idx >= 2:
                    S.wait(TE, pre + "_ev", S.n(pre + "_ev") - 1)
                if not first:
                    wait_rhs(TE, pr)
                if u_idx == 0:
                    wait_w(TE)
                last = None
                for h2 in range(2):
                    cs = h2 * W
                    for i in range(NCH):
                        if first:
                            chunk_wait(TE, pr, i, cs)
                        last = TE.matmul(pb[:, cs:cs + W],
                                         w_sb[:, i * D + j * P:i * D + (j + 1) * P],
                                         rhs_sb[:, i * TPC + pr * 1024 + cs:
                                                i * TPC + pr * 1024 + cs + W],
                                         start=(i == 0),
                                         stop=(bias_row is None and i == NCH - 1))
                if bias_row is not None:
                    for h2 in range(2):
                        cs = h2 * W
                        last = TE.matmul(pb[:, cs:cs + W], bias_row[0:1, j * P:(j + 1) * P],
                                         onesw[0:1, cs:cs + W], start=False, stop=True)
                S.inc(pre + "_mm", last)
                consume(j, pr, pb)

    wf2_ready = load_weight(wf_in, wf2_sb)

    def o_consume(j, pr, pb):
        S.wait(VE, "o_mm")
        xs = x_st[:, j * TPC + pr * 1024:j * TPC + (pr + 1) * 1024]
        S.inc("o_ev", VE.scalar_tensor_tensor(xs, pb[:, :], bo_sb[:, j:j + 1], xs,
                                              ALU.add, ALU.add))

    def wait_ctx(eng, pr):
        eng.wait_ge(S.get("n_ev")[0], nrm_half[pr])

    def chase_nrm(eng, pr, i, cs):
        eng.wait_ge(S.get("n_ev")[0], 8 * pr + i + 1)

    def wait_wo(eng):
        eng.wait_ge(S.get("s_wc")[0], wo_ready)

    emit_proj4(w_a, ctx_sb, o_consume, "o", wait_ctx, wait_wo, pr_outer=True,
               chunk_wait=chase_nrm)
    o_ev_done = S.n("o_ev")
    o_mm_done = S.n("o_mm")

    def src_wait_x2(eng, s, c):
        eng.wait_ge(S.get("o_ev")[0], min(8 * (s // 2) + c + 1, o_ev_done))
        eng.wait_ge(S.get("o_mm")[0], min(8 * (s // 2) + c + 1, o_mm_done))

    ln3_done = []

    def wait_u3(eng, pr):
        eng.wait_ge(S.get("l3_u")[0], ln3_done[2 * pr + 1])

    def chase_l3(eng, pr, i, cs):
        eng.wait_ge(S.get("l3_u")[0], 8 * (2 * pr + cs // W) + i + 1)

    def wait_wf2(eng):
        eng.wait_ge(S.get("s_wc")[0], wf2_ready)

    def f2_consume(j, pr, pb):
        ffn_consume("f2")(j, pr * 1024, pb)
        S.wait(SY, "f2_ev")
        S.dma("s_st", SY.dma_start(
            out=y_out[j * P:(j + 1) * P, pr * 1024:(pr + 1) * 1024],
            in_=x_st[:, j * TPC + pr * 1024:j * TPC + (pr + 1) * 1024]))

    for half in range(2):
        for s in (2 * half, 2 * half + 1):
            ln3_done += emit_ln4(src_wait_x2, "l3", u2_bf, slabs=[s])
        emit_proj4(wf2_sb, u2_bf, f2_consume, "f2", wait_u3, wait_wf2,
                   bias_row=fbrow, chunk_wait=chase_l3, prs=[half])
    S.wait(SY, "s_st")

    stD.close()
    stL3.close()
    stL2a.close()
    stack.close()
    return nc


_NC_CACHE = {}


def kernel(L0_attn_x, L0_attn_Wq, L0_attn_Wk, L0_attn_Wv, L0_attn_Wo, L0_attn_bo,
           L0_ffn_weight_fp8, L0_ffn_bias, L0_ffn_input_scale, L0_ffn_weight_scale):
    x = np.ascontiguousarray(np.asarray(L0_attn_x, dtype=np.float32))
    B = x.shape[0]
    assert x.shape == (4, SEQ, D)

    if "nc" not in _NC_CACHE:
        _NC_CACHE["nc"] = build_nc()
    nc = _NC_CACHE["nc"]

    wq = np.ascontiguousarray(np.asarray(L0_attn_Wq, dtype=np.float32))
    wk = np.ascontiguousarray(np.asarray(L0_attn_Wk, dtype=np.float32))
    wv = np.ascontiguousarray(np.asarray(L0_attn_Wv, dtype=np.float32))
    wo = np.ascontiguousarray(np.asarray(L0_attn_Wo, dtype=np.float32))
    wf_t = np.ascontiguousarray(np.asarray(L0_ffn_weight_fp8, dtype=np.float32).T)
    bo_rs = np.ascontiguousarray(np.asarray(L0_attn_bo, dtype=np.float32).reshape(NCH, P).T)
    fbias = np.ascontiguousarray(np.asarray(L0_ffn_bias, dtype=np.float32).reshape(1, D))
    in_s = np.asarray(L0_ffn_input_scale, dtype=np.float32).reshape(1, 1)
    w_s = np.asarray(L0_ffn_weight_scale, dtype=np.float32).reshape(1, 1)

    tok_idx = {0: list(range(0, 1024)) + list(range(3072, 4096)),
               1: list(range(1024, 3072))}
    in_maps = []
    for core in range(8):
        b, parity = core // 2, core % 2
        xs = np.ascontiguousarray(x[b, tok_idx[parity], :].T)
        in_maps.append({
            "x_fm": xs, "wq": wq, "wk": wk, "wv": wv, "wo": wo, "wf_t": wf_t,
            "bo_rs": bo_rs, "ffn_bias": fbias, "in_scale": in_s, "w_scale": w_s,
            "parity": np.array([[parity]], dtype=np.int32),
        })

    res = run_bass_kernel_spmd(nc, in_maps, core_ids=list(range(8)))

    out = np.empty((B, SEQ, D), dtype=np.float32)
    for core in range(8):
        b, parity = core // 2, core % 2
        out[b, tok_idx[parity], :] = res.results[core]["y_fm"].T
    return out


# version 40
# speedup vs baseline: 1.0122x; 1.0122x over previous
import contextlib
import numpy as np
import concourse.bass as bass
import concourse.mybir as mybir
from concourse.bass_utils import run_bass_kernel_spmd

F32 = mybir.dt.float32
BF16 = mybir.dt.bfloat16
I32 = mybir.dt.int32
ACTF = mybir.ActivationFunctionType
ALU = mybir.AluOpType

P = 128
D = 1024
NCH = 8
TPC = 2048
W = 512
NSLAB = 4
HP = 8
SEQ = 4096
C_VELT = float(2 ** 20 + 1)
EPS = 1e-5

ARM_BASE = {0: (0, 512, 3072, 3584), 1: (1024, 1536, 2048, 2560)}


def kb_map(t):
    if t < 8:
        return 0, t
    if t < 24:
        return 1, t - 8
    return 0, t - 16


class Sems:
    def __init__(self, nc, stack):
        self.nc, self.stack, self.d = nc, stack, {}

    def get(self, name):
        if name not in self.d:
            self.d[name] = [self.stack.enter_context(self.nc.semaphore(name)), 0]
        return self.d[name]

    def inc(self, name, instr, k=1):
        e = self.get(name)
        instr.then_inc(e[0], k)
        e[1] += k
        return e[1]

    def dma(self, name, instr):
        return self.inc(name, instr, 16)

    def n(self, name):
        return self.get(name)[1]

    def wait(self, eng, name, val=None):
        e = self.get(name)
        v = e[1] if val is None else val
        if v > 0:
            eng.wait_ge(e[0], v)

    def snapshot(self):
        return {k: v[1] for k, v in self.d.items()}

    def restore(self, snap):
        for k, v in self.d.items():
            v[1] = snap.get(k, 0)


def build_nc():
    nc = bass.Bass("TRN2", target_bir_lowering=False, detect_race_conditions=False)
    stack = contextlib.ExitStack()

    x_in = nc.dram_tensor("x_fm", [D, TPC], F32, kind="ExternalInput").ap()
    wq_in = nc.dram_tensor("wq", [D, D], F32, kind="ExternalInput").ap()
    wk_in = nc.dram_tensor("wk", [D, D], F32, kind="ExternalInput").ap()
    wv_in = nc.dram_tensor("wv", [D, D], F32, kind="ExternalInput").ap()
    wo_in = nc.dram_tensor("wo", [D, D], F32, kind="ExternalInput").ap()
    wf_in = nc.dram_tensor("wf_t", [D, D], F32, kind="ExternalInput").ap()
    bo_in = nc.dram_tensor("bo_rs", [P, NCH], F32, kind="ExternalInput").ap()
    fb_in = nc.dram_tensor("ffn_bias", [1, D], F32, kind="ExternalInput").ap()
    is_in = nc.dram_tensor("in_scale", [1, 1], F32, kind="ExternalInput").ap()
    ws_in = nc.dram_tensor("w_scale", [1, 1], F32, kind="ExternalInput").ap()
    par_in = nc.dram_tensor("parity", [1, 1], I32, kind="ExternalInput").ap()
    y_out = nc.dram_tensor("y_fm", [D, TPC], F32, kind="ExternalOutput").ap()

    k_bounce = nc.dram_tensor("k_bounce", [HP, P, TPC], BF16)
    v_bounce = nc.dram_tensor("v_bounce", [HP, P, 2080], BF16)
    k_gath = nc.dram_tensor("k_gath", [HP, 2, P, TPC], BF16)
    v_gath = nc.dram_tensor("v_gath", [HP, 2, P, 2080], BF16)

    def sbuf(st, name, shape, dt, side="left"):
        return st.enter_context(nc.sbuf_tensor(name, shape, dt, side=side))

    x_st = sbuf(stack, "x_st", [P, NCH * TPC], F32)
    w_a = sbuf(stack, "w_a", [P, NCH * D], BF16)
    tri = sbuf(stack, "tri", [P, P], BF16)
    e_sel = sbuf(stack, "e_sel", [16, NCH * P], BF16)
    ones128 = sbuf(stack, "ones128", [1, P], F32)
    inv1024 = sbuf(stack, "inv1024", [P, 1], BF16)
    onescol = sbuf(stack, "onescol", [P, 1], BF16)
    onesw = sbuf(stack, "onesw", [1, D], BF16)
    fbrow = sbuf(stack, "fbrow", [1, D], BF16)
    epsb = sbuf(stack, "epsb", [1, 1], F32)
    bo_sb = sbuf(stack, "bo_sb", [P, NCH], F32)
    sc_sb = sbuf(stack, "sc_sb", [1, 4], F32)
    rinsc = sbuf(stack, "rinsc", [P, 1], F32)
    ah_bc = sbuf(stack, "ah_bc", [P, 1], F32)
    par_sb = sbuf(stack, "par_sb", [1, 1], I32)
    wtmp = sbuf(stack, "wtmp", [P, 3 * W], F32)

    stR = contextlib.ExitStack()
    q_fm = sbuf(stR, "q_fm", [P, NCH * TPC], BF16, side="right")

    stL1 = contextlib.ExitStack()
    wf_sb = sbuf(stL1, "wf_sb", [P, NCH * D], BF16)
    u_bf = sbuf(stL1, "u_bf", [P, NCH * TPC], BF16)
    shad = sbuf(stL1, "shad", [P, NCH * W], BF16)
    lnf_a = sbuf(stL1, "lnf_a", [P, 2 * W], F32)
    lnf_b = sbuf(stL1, "lnf_b", [P, 2 * W], F32)
    strip_a = sbuf(stL1, "strip_a", [1, W], F32)
    strip_b = sbuf(stL1, "strip_b", [1, W], F32)
    k_stage = sbuf(stL1, "k_stage", [P, TPC], BF16)
    v_stage = sbuf(stL1, "v_stage", [P, 2080], BF16)
    iot = sbuf(stL1, "iot", [P, P], F32)
    fb_sb = sbuf(stL1, "fb_sb", [1, D], F32)

    S = Sems(nc, stack)
    TE, VE, SE, GE, SY = nc.tensor, nc.vector, nc.scalar, nc.gpsimd, nc.sync

    x_slab_loaded = []
    for sl in range(NSLAB):
        for c in range(NCH):
            S.dma("s_in", SY.dma_start(
                out=x_st[:, c * TPC + sl * W:c * TPC + (sl + 1) * W],
                in_=x_in[c * P:(c + 1) * P, sl * W:(sl + 1) * W]))
        x_slab_loaded.append(S.n("s_in"))
    x_loaded = S.n("s_in")
    S.dma("s_in", SY.dma_start(out=par_sb[:, :], in_=par_in))
    S.dma("s_in", SY.dma_start(out=bo_sb[:, :], in_=bo_in))
    S.dma("s_in", SY.dma_start(out=fb_sb[:, :], in_=fb_in))
    S.dma("s_in", SY.dma_start(out=sc_sb[:, 0:1], in_=is_in))
    S.dma("s_in", SY.dma_start(out=sc_sb[:, 1:2], in_=ws_in))
    smalls_loaded = S.n("s_in")

    def load_weight(src, dst):
        lanes = (SY, GE, SE)
        for piece in range(16):
            cc, half = piece // 2, piece % 2
            wb = wtmp[:, (piece % 3) * W:(piece % 3 + 1) * W]
            lane = lanes[piece % 3]
            if piece >= 3:
                S.wait(lane, "s_wc", S.n("s_wc") - 2)
            else:
                S.wait(lane, "s_wc")
            S.dma("s_w%d" % (piece % 3), lane.dma_start(
                out=wb, in_=src[cc * P:(cc + 1) * P, half * W:(half + 1) * W]))
            VE.wait_ge(S.get("s_w%d" % (piece % 3))[0], S.n("s_w%d" % (piece % 3)))
            S.inc("s_wc", VE.tensor_copy(dst[:, cc * D + half * W:cc * D + (half + 1) * W],
                                         wb))
        return S.n("s_wc")

    wf_ready = load_weight(wf_in, wf_sb)

    S.inc("s_set", GE.memset(ones128[:, :], 1.0))
    S.inc("s_set", GE.memset(inv1024[:, :], 1.0 / 1024.0))
    S.inc("s_set", GE.memset(onescol[:, :], 1.0))
    S.inc("s_set", GE.memset(onesw[:, :], 1.0))
    S.inc("s_set", GE.memset(
        v_stage[:, :].rearrange("p (a b) -> p a b", b=65)[:, :, 64:65], 1.0))
    S.inc("s_set", GE.memset(epsb[:, :], EPS))
    S.inc("s_set", GE.iota(iot[:, :], pattern=[[1, P]], base=0, channel_multiplier=-1,
                           allow_small_or_imprecise_dtypes=True))
    S.inc("s_set", GE.iota(lnf_a[0:16, 0:NCH * P].rearrange("p (h f m) -> p h f m", h=NCH, f=2),
                           pattern=[[-2, NCH], [-1, 2], [0, 64]], base=0, channel_multiplier=1,
                           allow_small_or_imprecise_dtypes=True))
    S.wait(VE, "s_set")
    S.inc("s_set", VE.tensor_single_scalar(tri[:, :], iot[:, :], 0.0, ALU.is_ge))
    S.inc("s_set", VE.tensor_single_scalar(e_sel[:, :], lnf_a[0:16, 0:NCH * P], 0.0,
                                           ALU.is_equal))
    setup_done = S.n("s_set")

    stA = contextlib.ExitStack()
    psA = lambda name, shape: stA.enter_context(nc.psum_tensor(name, shape, F32))
    ps_stats = psA("ps_stats", [P, 1024])
    ps_rb = psA("ps_rb", [P, W])
    ps_bb = psA("ps_bb", [P, W])
    ps_pj = [psA("ps_pj0", [P, 1024]), psA("ps_pj1", [P, 1024])]

    sA, sB = strip_a[0:1, 0:W], strip_b[0:1, 0:W]
    sC, sD = lnf_a[0:1, 0:W], lnf_a[0:1, W:2 * W]
    sE, sF = lnf_b[0:1, 0:W], lnf_b[0:1, W:2 * W]
    S.inc("s_set", GE.memset(strip_a[:, :], 1.0))
    S.wait(VE, "s_set")
    VE.wait_ge(S.get("s_in")[0], smalls_loaded)
    VE.tensor_scalar_mul(sB, sA, sc_sb[0:1, 0:1])
    VE.tensor_scalar_mul(sC, sA, sc_sb[0:1, 1:2])
    VE.tensor_mul(sD, sB, sC)
    VE.tensor_scalar_mul(sE, sD, 0.5)
    VE.tensor_single_scalar(sF, sD, 0.0, ALU.is_ge)
    VE.tensor_scalar(sF, sF, 2.0, -1.0, ALU.mult, ALU.add)
    S.inc("s_rc", VE.scalar_tensor_tensor(sA, sD, -1.0, sD, ALU.mult, ALU.max))
    S.wait(SE, "s_rc")
    SE.activation(sA, sA, ACTF.Ln, scale=1.0)
    S.inc("s_rc", SE.activation(sA, sA, ACTF.Exp, scale=-1.0))
    S.wait(VE, "s_rc")
    VE.tensor_mul(sA, sA, sF)
    S.inc("s_set", VE.tensor_scalar_mul(fbrow[:, :], fb_sb[:, :], sA[0:1, 0:1]))
    VE.tensor_single_scalar(sC, sB, 0.0, ALU.is_ge)
    VE.tensor_scalar(sC, sC, 2.0, -1.0, ALU.mult, ALU.add)
    S.inc("s_rc", VE.scalar_tensor_tensor(sB, sB, -1.0, sB, ALU.mult, ALU.max))
    S.wait(SE, "s_rc")
    SE.activation(sB, sB, ACTF.Ln, scale=1.0)
    S.inc("s_rc", SE.activation(sB, sB, ACTF.Exp, scale=-1.0))
    S.wait(VE, "s_rc")
    S.inc("s_rc", VE.tensor_mul(sB, sB, sC))
    S.wait(TE, "s_rc")
    TE.wait_ge(S.get("s_set")[0], S.n("s_set") - 1)
    TE.matmul(ps_stats[:, 2:3], ones128[0:1, :], sE[0:1, 0:1], start=True, stop=True)
    S.inc("s_mm0", TE.matmul(ps_stats[:, 3:4], ones128[0:1, :], sB[0:1, 0:1],
                             start=True, stop=True))
    S.wait(VE, "s_mm0")
    VE.tensor_copy(ah_bc[:, :], ps_stats[:, 2:3])
    S.inc("s_set", VE.tensor_copy(rinsc[:, :], ps_stats[:, 3:4]))
    setup_done = S.n("s_set")
    ah_ready = S.n("s_set")

    def make_ln(src_wait, quant, pre, out_bf):
        def ln_a(s):
            col = s * W
            if s == 0:
                S.wait(TE, "s_set", setup_done)
                S.wait(SE, "s_set", setup_done)
                S.wait(VE, "s_set", setup_done)
            for c in range(NCH):
                src_wait(VE, s, c)
                S.inc(pre + "_cp", VE.tensor_copy(shad[:, c * W:(c + 1) * W],
                                                  x_st[:, c * TPC + col:c * TPC + col + W]))
            for c in range(NCH):
                TE.wait_ge(S.get(pre + "_cp")[0], NCH * s + c + 1)
                if S.n(pre + "_sr") > 0 and c == 0:
                    S.wait(TE, pre + "_sr")
                S.inc(pre + "_m1", TE.matmul(ps_stats[0:1, 0:W], inv1024[:, 0:1],
                                             shad[:, c * W:(c + 1) * W],
                                             start=(c == 0), stop=(c == NCH - 1)))

        def ln_b(s):
            for c in range(NCH):
                VE.wait_ge(S.get(pre + "_m1")[0], NCH * s + c + 1)
                S.inc(pre + "_sq", VE.tensor_mul(shad[:, c * W:(c + 1) * W],
                                                 shad[:, c * W:(c + 1) * W],
                                                 shad[:, c * W:(c + 1) * W]))
            for c in range(NCH):
                TE.wait_ge(S.get(pre + "_sq")[0], NCH * s + c + 1)
                S.inc(pre + "_m2", TE.matmul(ps_stats[0:1, W:2 * W], inv1024[:, 0:1],
                                             shad[:, c * W:(c + 1) * W],
                                             start=(c == 0), stop=(c == NCH - 1)))
            S.wait(VE, pre + "_m2")
            VE.tensor_copy(strip_b[:, :], ps_stats[0:1, 0:W])
            VE.tensor_mul(strip_a[:, :], strip_b[:, :], strip_b[:, :])
            S.inc(pre + "_va", VE.tensor_sub(strip_a[:, :], ps_stats[0:1, W:2 * W],
                                             strip_a[:, :]))
            S.wait(SE, pre + "_va")
            SE.activation(strip_b[:, :], strip_a[:, :], ACTF.Ln, bias=epsb[0:1, 0:1], scale=1.0)
            S.inc(pre + "_rs", SE.activation(strip_b[:, :], strip_b[:, :], ACTF.Exp, scale=-0.5))
            S.wait(VE, pre + "_rs")
            if quant:
                VE.wait_ge(S.get("s_set")[0], setup_done)
                VE.tensor_scalar_mul(strip_b[:, :], strip_b[:, :], rinsc[0:1, 0:1])
            S.inc(pre + "_sr", VE.scalar_tensor_tensor(strip_a[:, :], ps_stats[0:1, 0:W], -1.0,
                                                       strip_b[:, :], ALU.mult, ALU.mult))

        def ln_bc(s):
            S.wait(TE, pre + "_sr")
            if S.n(pre + "_u") > 0:
                S.wait(TE, pre + "_u")
            TE.matmul(ps_rb[:, :], ones128[0:1, :], strip_b[0:1, :], start=True, stop=True)
            S.inc(pre + "_bc", TE.matmul(ps_bb[:, :], ones128[0:1, :], strip_a[0:1, :],
                                         start=True, stop=True))

        def ln_apply(s):
            col = s * W
            S.wait(VE, pre + "_bc")
            for c in range(NCH):
                a = lnf_a[:, (c % 2) * W:(c % 2 + 1) * W]
                b = lnf_b[:, (c % 2) * W:(c % 2 + 1) * W]
                xs = x_st[:, c * TPC + col:c * TPC + col + W]
                ub = out_bf[:, c * TPC + col:c * TPC + col + W]
                if quant:
                    if c >= 2:
                        VE.wait_ge(S.get(pre + "_u")[0], S.n(pre + "_u") - 1)
                    VE.tensor_mul(a, xs, ps_rb[:, :])
                    VE.tensor_add(a, a, ps_bb[:, :])
                    S.inc(pre + "_cl", VE.tensor_scalar(a, a, -448.0, 448.0, ALU.max, ALU.min))
                    S.wait(SE, pre + "_cl")
                    S.inc(pre + "_t", SE.activation(b, a, ACTF.Copy, scale=C_VELT))
                    S.wait(VE, pre + "_t")
                    VE.tensor_sub(a, b, a)
                    S.inc(pre + "_u", VE.tensor_sub(ub, b, a))
                else:
                    VE.tensor_mul(a, xs, ps_rb[:, :])
                    S.inc(pre + "_u", VE.tensor_add(ub, a, ps_bb[:, :]))
            return S.n(pre + "_u")

        return ln_a, ln_b, ln_bc, ln_apply

    def emit_proj(w_sb, rhs_sb, consume, pre, wait_rhs, wait_w, bias_row=None,
                  pr_outer=False, units=None, ucols=1024, chunk_wait=None):
        if units is None:
            units = ([(j, pr * 1024) for pr in range(2) for j in range(NCH)] if pr_outer
                     else [(j, pr * 1024) for j in range(NCH) for pr in range(2)])
        for ui, (j, coff) in enumerate(units):
            u_idx = S.n(pre + "_mm")
            pb = ps_pj[u_idx % 2]
            if u_idx >= 2:
                S.wait(TE, pre + "_ev", S.n(pre + "_ev") - 1)
            if chunk_wait is None or ui > 0:
                wait_rhs(TE, coff)
            if u_idx == 0:
                wait_w(TE)
            last = None
            for cs in range(0, ucols, W):
                for i in range(NCH):
                    if chunk_wait is not None and ui == 0:
                        chunk_wait(TE, coff, i, cs)
                    last = TE.matmul(pb[:, cs:cs + W],
                                     w_sb[:, i * D + j * P:i * D + (j + 1) * P],
                                     rhs_sb[:, i * TPC + coff + cs:
                                            i * TPC + coff + cs + W],
                                     start=(i == 0),
                                     stop=(bias_row is None and i == NCH - 1))
            if bias_row is not None:
                for cs in range(0, ucols, W):
                    last = TE.matmul(pb[:, cs:cs + W], bias_row[0:1, j * P:(j + 1) * P],
                                     onesw[0:1, cs:cs + W], start=False, stop=True)
            S.inc(pre + "_mm", last)
            consume(j, coff, pb)

    def src_wait_x0(eng, s, c):
        eng.wait_ge(S.get("s_in")[0], 16 * (8 * s + c + 1))

    def ffn_consume(pre, ucols=1024):
        def consume(j, coff, pb):
            S.wait(VE, pre + "_mm")
            VE.wait_ge(S.get("s_set")[0], ah_ready)
            xs = x_st[:, j * TPC + coff:j * TPC + coff + ucols]
            S.inc(pre + "_ev", VE.scalar_tensor_tensor(xs, pb[:, 0:ucols], ah_bc[:, 0:1],
                                                       xs, ALU.mult, ALU.add))
        return consume

    ln1_done = []

    def wait_u1(eng, coff):
        eng.wait_ge(S.get("l1_u")[0], ln1_done[coff // W])

    def wait_wf(eng):
        eng.wait_ge(S.get("s_wc")[0], wf_ready)
        eng.wait_ge(S.get("s_set")[0], setup_done)

    def chase_l1(eng, coff, i, cs):
        eng.wait_ge(S.get("l1_u")[0], 8 * (coff // W) + i + 1)

    f1c = ffn_consume("f1", W)
    l1a, l1b, l1bc, l1ap = make_ln(src_wait_x0, True, "l1", u_bf)
    l1a(0)
    l1b(0)
    for s in range(NSLAB):
        l1bc(s)
        if s + 1 < NSLAB:
            l1a(s + 1)
        ln1_done.append(l1ap(s))
        emit_proj(wf_sb, u_bf, f1c, "f1", wait_u1, wait_wf, bias_row=fbrow,
                  units=[(j, s * W) for j in range(NCH)], ucols=W,
                  chunk_wait=chase_l1)
        if s + 1 < NSLAB:
            l1b(s + 1)
    f1_ev_done = S.n("f1_ev")

    wk_ready = load_weight(wk_in, w_a)

    f1_mm_done = S.n("f1_mm")

    def src_wait_x1(eng, s, c):
        eng.wait_ge(S.get("f1_ev")[0], min(8 * s + c + 1, f1_ev_done))
        eng.wait_ge(S.get("f1_mm")[0], min(8 * s + c + 1, f1_mm_done))

    ln2_done = []

    def wait_xln2(eng, coff):
        eng.wait_ge(S.get("l2_u")[0], ln2_done[2 * (coff // 1024) + 1])

    def k_consume(j, coff, pb):
        pr = coff // 1024
        slot = j % 2
        u = S.n("k_ev")
        S.wait(VE, "k_mm")
        if u >= 2:
            VE.wait_ge(S.get("kv_od")[0], 16 * (u - 1))
        S.inc("k_ev", VE.tensor_copy(k_stage[:, slot * 1024:(slot + 1) * 1024],
                                     pb[:, :]))
        S.wait(GE, "k_ev")
        S.dma("kv_od", GE.dma_start(out=k_bounce[j, :, pr * 1024:(pr + 1) * 1024],
                                    in_=k_stage[:, slot * 1024:(slot + 1) * 1024]))
        if pr == 1:
            GE.wait_ge(S.get("kv_od")[0], 16 * S.n("k_ev"))
            S.inc("s_ck", GE.collective_compute(
                "AllGather", ALU.bypass,
                replica_groups=[[0, 1], [2, 3], [4, 5], [6, 7]],
                ins=[k_bounce[j, :, :].opt()],
                outs=[k_gath[j, :, :, :].opt()],
            ))

    def wait_wk(eng):
        eng.wait_ge(S.get("s_wc")[0], wk_ready)
        S.wait(eng, "f1_ev")

    def chase_l2(eng, coff, i, cs):
        eng.wait_ge(S.get("l2_u")[0], 8 * (2 * (coff // 1024) + cs // W) + i + 1)

    l2a, l2b, l2bc, l2ap = make_ln(src_wait_x1, False, "l2", u_bf)
    l2a(0)
    l2b(0)
    for s in range(NSLAB):
        l2bc(s)
        if s + 1 < NSLAB:
            l2a(s + 1)
        ln2_done.append(l2ap(s))
        if s % 2 == 1:
            emit_proj(w_a, u_bf, k_consume, "k", wait_xln2, wait_wk,
                      units=[(j, (s // 2) * 1024) for j in range(NCH)],
                      chunk_wait=chase_l2)
        if s + 1 < NSLAB:
            l2b(s + 1)

    SY.wait_ge(S.get("f1_mm")[0], f1_mm_done)
    wv_ready = load_weight(wv_in, wf_sb)

    v_stage_view = v_stage[:, :].rearrange("p (hp t h c) -> p hp t h c", hp=HP, t=2, h=2)
    for e8 in range(8):
        for tt in range(2):
            t = e8 * 2 + tt
            s = t // 4
            pb = ps_pj[t % 2]
            if t >= 2:
                S.wait(TE, "v_ev", S.n("v_ev") - 1)
            TE.wait_ge(S.get("l2_u")[0], ln2_done[s])
            if t == 0:
                TE.wait_ge(S.get("s_wc")[0], wv_ready)
                S.wait(TE, "k_ev")
            last = None
            for i in range(NCH):
                for h2 in range(2):
                    cs = h2 * W
                    last = TE.matmul(pb[:, cs:cs + W],
                                     u_bf[:, i * TPC + t * P:i * TPC + (t + 1) * P],
                                     wf_sb[:, i * D + cs:i * D + cs + W],
                                     start=(i == 0), stop=(i == NCH - 1))
            S.inc("v_mm", last)
            S.wait(VE, "v_mm")
            if e8 >= 1 and tt == 0:
                VE.wait_ge(S.get("kv_vd")[0], 16 * (8 * (e8 - 1) + 8))
            S.inc("v_ev", VE.tensor_copy(
                v_stage_view[:, :, tt, :, 0:64],
                pb[:, :].rearrange("p (hp h c) -> p hp h c", hp=HP, h=2)))
        S.wait(GE, "v_ev")
        for hp in range(HP):
            S.dma("kv_vd", GE.dma_start(
                out=v_bounce[hp, :, e8 * 260:(e8 + 1) * 260],
                in_=v_stage[:, hp * 260:(hp + 1) * 260]))

    SY.wait_ge(S.get("k_mm")[0], S.n("k_mm"))
    wq_ready = load_weight(wq_in, w_a)

    S.wait(GE, "kv_vd")
    for hp in range(HP):
        S.inc("s_cv", GE.collective_compute(
            "AllGather", ALU.bypass,
            replica_groups=[[0, 1], [2, 3], [4, 5], [6, 7]],
            ins=[v_bounce[hp, :, :].opt()],
            outs=[v_gath[hp, :, :, :].opt()],
        ))

    def q_consume(j, coff, pb):
        S.wait(VE, "q_mm")
        S.inc("q_ev", VE.tensor_copy(q_fm[:, j * TPC + coff:j * TPC + coff + 1024],
                                     pb[:, :]))

    def wait_wq(eng):
        eng.wait_ge(S.get("s_wc")[0], wq_ready)
        S.wait(eng, "v_ev")

    emit_proj(w_a, u_bf, q_consume, "q", wait_xln2, wait_wq)

    SY.wait_ge(S.get("q_mm")[0], S.n("q_mm"))
    wo_ready = load_weight(wo_in, w_a)

    stA.close()
    stL1.close()

    stL2a = contextlib.ExitStack()
    ctx_sb = sbuf(stL2a, "ctx_sb", [P, HP * TPC], BF16)
    stL2b = contextlib.ExitStack()
    sums_sb = sbuf(stL2b, "sums_sb", [16, TPC], BF16)
    sstrip = sbuf(stL2b, "sstrip", [1, 2 * W], BF16)
    kv_sb = sbuf(stL2b, "kv_sb", [P, 2 * 8256], BF16)
    pt_sb = sbuf(stL2b, "pt_sb", [P, 2 * 3072], BF16)

    for eng in (TE, VE, SE, GE, SY):
        eng.wait_ge(S.get("s_in")[0], smalls_loaded)
    parv = nc.values_load(par_sb[0:1, 0:1], min_val=0, max_val=1,
                          skip_runtime_bounds_check=True)

    stB = contextlib.ExitStack()
    psB = lambda name, shape: stB.enter_context(nc.psum_tensor(name, shape, F32))
    sc_ps = psB("sc_ps", [P, 3072])
    ctx_psA = psB("ctx_psA", [P, W])
    ctx_psB = psB("ctx_psB", [P, W])
    S.inc("ps_i", VE.memset(sc_ps[:, :], 0.0))

    def emit_arm(arm):
        base = ARM_BASE[arm]
        S.wait(TE, "ps_i")
        av_after_hp = {}
        ng = [0]
        pend = []
        LAG = 2

        def do_scores(hp, s, h, gkbs):
            qcol = hp * TPC + s * W
            TE.wait_ge(S.get("a_kv")[0], k_ready_of[hp])
            if ng[0] >= 2:
                S.wait(TE, "a_exp", ng[0] - 1)
            sb_ = (ng[0] % 2) * 1536
            K_all = K_all_of[hp]
            last = None
            for si, kb in enumerate(gkbs):
                off = max(0, P * kb - base[s])
                slot, tile = kb_map(kb)
                kcol = slot * TPC + P * tile
                last = TE.matmul(
                    sc_ps[:, sb_ + si * W + off:sb_ + (si + 1) * W],
                    K_all[h * 64:h * 64 + 64, kcol:kcol + P],
                    q_fm[h * 64:h * 64 + 64, qcol + off:qcol + W],
                    start=True, stop=True, tile_position=(h * 64, 0))
            S.inc("a_sc", last)

        def do_exp(ngk):
            S.wait(SE, "a_sc")
            if ng[0] >= 4:
                S.wait(SE, "a_av", ng[0] - 3)
            sb_ = (ng[0] % 2) * 1536
            pb_ = (ng[0] % 4) * 1536
            wexp = 512 * ngk
            S.inc("a_exp", SE.activation(pt_sb[:, pb_:pb_ + wexp],
                                         sc_ps[:, sb_:sb_ + wexp],
                                         ACTF.Exp, scale=0.125))

        def do_masks(s, h, gkbs, my_ng, n_kb):
            diag = [(si, kb) for si, kb in enumerate(gkbs) if kb >= n_kb - 4]
            if not diag:
                return S.n("a_msk")
            VE.wait_ge(S.get("a_exp")[0], my_ng + 1)
            pb_ = (my_ng % 4) * 1536
            last = None
            for si, kb in diag:
                off = max(0, P * kb - base[s])
                pc = pb_ + si * W + off
                last = VE.tensor_mul(pt_sb[:, pc:pc + P], pt_sb[:, pc:pc + P],
                                     tri[:, :])
            S.inc("a_msk", last)
            return S.n("a_msk")

        def do_av(hp, s, h, gkbs, my_ng, msk_cnt):
            TE.wait_ge(S.get("a_exp")[0], my_ng + 1)
            TE.wait_ge(S.get("a_kv")[0], kv_full_of[hp])
            if msk_cnt > 0:
                S.wait(TE, "a_msk", msk_cnt)
            if gkbs[0] == 0 and S.n("a_cx") > 0:
                S.wait(TE, "a_cx")
            pb_ = (my_ng % 4) * 1536
            ctx = ctx_psA if h == 0 else ctx_psB
            last = None
            for si, kb in enumerate(gkbs):
                off = max(0, P * kb - base[s])
                slot, tile = kb_map(kb)
                vcol = slot * 2080 + 130 * tile
                pc = pb_ + si * W + off
                last = TE.matmul(ctx[0:65, off:W],
                                 V_all_of[hp][:, vcol + h * 65:vcol + h * 65 + 65],
                                 pt_sb[:, pc:pc + (W - off)],
                                 start=(kb == 0), stop=False,
                                 tile_position=(0, 0), skip_group_check=True)
            S.inc("a_av", last)

        def do_evac(hp, s):
            S.wait(VE, "a_av")
            S.wait(VE, "a_sm")
            ccol = hp * TPC + s * W
            VE.tensor_copy(ctx_sb[0:64, ccol:ccol + W], ctx_psA[0:64, :])
            VE.tensor_copy(ctx_sb[64:128, ccol:ccol + W], ctx_psB[0:64, :])
            VE.tensor_copy(sstrip[0:1, 0:W], ctx_psA[64:65, :])
            S.inc("a_cx", VE.tensor_copy(sstrip[0:1, W:2 * W], ctx_psB[64:65, :]))
            S.wait(SY, "a_cx")
            S.dma("a_sm", SY.dma_start(out=sums_sb[2 * hp:2 * hp + 1, s * W:(s + 1) * W],
                                       in_=sstrip[0:1, 0:W]))
            S.dma("a_sm", SY.dma_start(out=sums_sb[2 * hp + 1:2 * hp + 2, s * W:(s + 1) * W],
                                       in_=sstrip[0:1, W:2 * W]))

        def drain_one():
            hp_, s_, h_, gkbs_, my_, mc_, lastslab = pend.pop(0)
            do_av(hp_, s_, h_, gkbs_, my_, mc_)
            if lastslab:
                do_evac(hp_, s_)
                if s_ == NSLAB - 1:
                    av_after_hp[hp_] = S.n("a_av")

        k_ready_of, kv_full_of, K_all_of, V_all_of = {}, {}, {}, {}

        def load_hp(hp):
            buf = hp % 2
            kvb = buf * 8256
            S.wait(GE, "s_ck", hp + 1)
            if hp >= 2:
                GE.wait_ge(S.get("a_av")[0], av_after_hp[hp - 2])
            for slot in range(2):
                S.dma("a_kv", GE.dma_start(
                    out=kv_sb[:, kvb + slot * TPC:kvb + (slot + 1) * TPC],
                    in_=k_gath[hp, slot, :, :]))
            k_ready_of[hp] = S.n("a_kv")
            S.wait(GE, "s_cv", hp + 1)
            for slot in range(2):
                S.dma("a_kv", GE.dma_start(
                    out=kv_sb[:, kvb + 4096 + slot * 2080:kvb + 4096 + (slot + 1) * 2080],
                    in_=v_gath[hp, slot, :, :]))
            kv_full_of[hp] = S.n("a_kv")
            K_all_of[hp] = kv_sb[:, kvb:kvb + 4096]
            V_all_of[hp] = kv_sb[:, kvb + 4096:kvb + 8256]

        cur_hp = -1
        for hp in range(HP):
            for s in range(NSLAB):
                n_kb = (base[s] + W) // P
                G = [list(range(i, min(i + 3, n_kb))) for i in range(0, n_kb, 3)]
                seq = [(h, gkbs) for gkbs in G for h in (0, 1)]
                for li, (h, gkbs) in enumerate(seq):
                    if hp != cur_hp:
                        load_hp(hp)
                        cur_hp = hp
                    my = ng[0]
                    do_scores(hp, s, h, gkbs)
                    do_exp(len(gkbs))
                    mc = do_masks(s, h, gkbs, my, n_kb)
                    pend.append((hp, s, h, gkbs, my, mc, li == len(seq) - 1))
                    ng[0] += 1
                    if li % 2 == 1:
                        while len(pend) > LAG:
                            drain_one()
        while pend:
            drain_one()
        S.wait(VE, "a_cx")
        S.inc("a_done", VE.engine_nop())

    snap = S.snapshot()
    with nc.If(parv):
        emit_arm(1)
    nc.end_ifs()
    arm1 = S.n("a_done")
    S.restore(snap)
    with nc.If(1 - parv):
        emit_arm(0)
    nc.end_ifs()
    assert S.n("a_done") == arm1 == snap.get("a_done", 0) + 1

    stB.close()

    stC = contextlib.ExitStack()
    ps_nrm = stC.enter_context(nc.psum_tensor("ps_nrm", [P, TPC], F32))
    for eng in (GE, SE, VE, TE):
        S.wait(eng, "a_done")
    S.wait(SE, "a_sm")
    SE.activation(sums_sb[:, :], sums_sb[:, :], ACTF.Ln, scale=1.0)
    S.inc("n_rc", SE.activation(sums_sb[:, :], sums_sb[:, :], ACTF.Exp, scale=-1.0))
    nrm_half = []
    for step in range(2 * HP):
        pr, hp = step // HP, step % HP
        nb = (step % 2) * 1024
        S.wait(TE, "n_rc")
        if step >= 2:
            S.wait(TE, "n_ev", S.n("n_ev") - 1)
        last = None
        for h4 in range(2):
            cs = h4 * W
            last = TE.matmul(ps_nrm[:, nb + cs:nb + cs + W],
                             e_sel[:, hp * P:(hp + 1) * P],
                             sums_sb[0:16, pr * 1024 + cs:pr * 1024 + cs + W],
                             start=True, stop=True)
        S.inc("n_mm", last)
        S.wait(VE, "n_mm")
        ccol = hp * TPC + pr * 1024
        S.inc("n_ev", VE.tensor_mul(ctx_sb[:, ccol:ccol + 1024],
                                    ctx_sb[:, ccol:ccol + 1024],
                                    ps_nrm[:, nb:nb + 1024]))
        if hp == HP - 1:
            nrm_half.append(S.n("n_ev"))
    stC.close()
    nrm_done = S.n("n_ev")

    stR.close()
    stL2b.close()

    stL3 = contextlib.ExitStack()
    u2_bf = sbuf(stL3, "u2_bf", [P, NCH * TPC], BF16)
    wf2_sb = sbuf(stL3, "wf2_sb", [P, NCH * D], BF16)
    shad2 = sbuf(stL3, "shad2", [P, NCH * W], BF16)
    lnf2_a = sbuf(stL3, "lnf2_a", [P, 2 * W], F32)
    lnf2_b = sbuf(stL3, "lnf2_b", [P, 2 * W], F32)
    strip2_a = sbuf(stL3, "strip2_a", [1, W], F32)
    strip2_b = sbuf(stL3, "strip2_b", [1, W], F32)

    stD = contextlib.ExitStack()
    psD = lambda name, shape: stD.enter_context(nc.psum_tensor(name, shape, F32))
    ps_stats2 = psD("ps_stats2", [P, 1024])
    ps_rb2 = psD("ps_rb2", [P, W])
    ps_bb2 = psD("ps_bb2", [P, W])
    ps_pj2 = [psD("ps_pj0b", [P, 1024]), psD("ps_pj1b", [P, 1024])]

    def emit_ln4(src_wait, pre, out_bf, slabs=None):
        done = []
        for s in (range(NSLAB) if slabs is None else slabs):
            col = s * W
            if s == 0:
                S.wait(TE, "s_set", setup_done)
                S.wait(SE, "s_set", setup_done)
                S.wait(VE, "s_set", setup_done)
            for c in range(NCH):
                src_wait(VE, s, c)
                S.inc(pre + "_cp", VE.tensor_copy(shad2[:, c * W:(c + 1) * W],
                                                  x_st[:, c * TPC + col:c * TPC + col + W]))
            base_cp = S.n(pre + "_cp") - NCH
            base_m1 = S.n(pre + "_m1")
            for c in range(NCH):
                TE.wait_ge(S.get(pre + "_cp")[0], base_cp + c + 1)
                if S.n(pre + "_sr") > 0 and c == 0:
                    S.wait(TE, pre + "_sr")
                S.inc(pre + "_m1", TE.matmul(ps_stats2[0:1, 0:W], inv1024[:, 0:1],
                                             shad2[:, c * W:(c + 1) * W],
                                             start=(c == 0), stop=(c == NCH - 1)))
            for c in range(NCH):
                VE.wait_ge(S.get(pre + "_m1")[0], base_m1 + c + 1)
                S.inc(pre + "_sq", VE.tensor_mul(shad2[:, c * W:(c + 1) * W],
                                                 shad2[:, c * W:(c + 1) * W],
                                                 shad2[:, c * W:(c + 1) * W]))
            base_sq = S.n(pre + "_sq") - NCH
            for c in range(NCH):
                TE.wait_ge(S.get(pre + "_sq")[0], base_sq + c + 1)
                S.inc(pre + "_m2", TE.matmul(ps_stats2[0:1, W:2 * W], inv1024[:, 0:1],
                                             shad2[:, c * W:(c + 1) * W],
                                             start=(c == 0), stop=(c == NCH - 1)))
            S.wait(VE, pre + "_m2")
            VE.tensor_copy(strip2_b[:, :], ps_stats2[0:1, 0:W])
            VE.tensor_mul(strip2_a[:, :], strip2_b[:, :], strip2_b[:, :])
            S.inc(pre + "_va", VE.tensor_sub(strip2_a[:, :], ps_stats2[0:1, W:2 * W],
                                             strip2_a[:, :]))
            S.wait(SE, pre + "_va")
            SE.activation(strip2_b[:, :], strip2_a[:, :], ACTF.Ln, bias=epsb[0:1, 0:1], scale=1.0)
            S.inc(pre + "_rs", SE.activation(strip2_b[:, :], strip2_b[:, :], ACTF.Exp, scale=-0.5))
            S.wait(VE, pre + "_rs")
            VE.wait_ge(S.get("s_set")[0], setup_done)
            VE.tensor_scalar_mul(strip2_b[:, :], strip2_b[:, :], rinsc[0:1, 0:1])
            S.inc(pre + "_sr", VE.scalar_tensor_tensor(strip2_a[:, :], ps_stats2[0:1, 0:W], -1.0,
                                                       strip2_b[:, :], ALU.mult, ALU.mult))
            S.wait(TE, pre + "_sr")
            if S.n(pre + "_u") > 0:
                S.wait(TE, pre + "_u")
            TE.matmul(ps_rb2[:, :], ones128[0:1, :], strip2_b[0:1, :], start=True, stop=True)
            S.inc(pre + "_bc", TE.matmul(ps_bb2[:, :], ones128[0:1, :], strip2_a[0:1, :],
                                         start=True, stop=True))
            S.wait(VE, pre + "_bc")
            for c in range(NCH):
                a = lnf2_a[:, (c % 2) * W:(c % 2 + 1) * W]
                b = lnf2_b[:, (c % 2) * W:(c % 2 + 1) * W]
                xs = x_st[:, c * TPC + col:c * TPC + col + W]
                ub = out_bf[:, c * TPC + col:c * TPC + col + W]
                if c >= 2:
                    VE.wait_ge(S.get(pre + "_u")[0], S.n(pre + "_u") - 1)
                VE.tensor_mul(a, xs, ps_rb2[:, :])
                VE.tensor_add(a, a, ps_bb2[:, :])
                S.inc(pre + "_cl", VE.tensor_scalar(a, a, -448.0, 448.0, ALU.max, ALU.min))
                S.wait(SE, pre + "_cl")
                S.inc(pre + "_t", SE.activation(b, a, ACTF.Copy, scale=C_VELT))
                S.wait(VE, pre + "_t")
                VE.tensor_sub(a, b, a)
                S.inc(pre + "_u", VE.tensor_sub(ub, b, a))
            done.append(S.n(pre + "_u"))
        return done

    def emit_proj4(w_sb, rhs_sb, consume, pre, wait_rhs, wait_w, bias_row=None,
                   pr_outer=False, chunk_wait=None, prs=None):
        order = ([(j, pr) for pr in (range(2) if prs is None else prs)
                  for j in range(NCH)] if (pr_outer or prs is not None)
                 else [(j, pr) for j in range(NCH) for pr in range(2)])
        for j, pr in order:
            if True:
                u_idx = S.n(pre + "_mm")
                pb = ps_pj2[u_idx % 2]
                first = chunk_wait is not None and j == 0
                if u_idx >= 2:
                    S.wait(TE, pre + "_ev", S.n(pre + "_ev") - 1)
                if not first:
                    wait_rhs(TE, pr)
                if u_idx == 0:
                    wait_w(TE)
                last = None
                for h2 in range(2):
                    cs = h2 * W
                    for i in range(NCH):
                        if first:
                            chunk_wait(TE, pr, i, cs)
                        last = TE.matmul(pb[:, cs:cs + W],
                                         w_sb[:, i * D + j * P:i * D + (j + 1) * P],
                                         rhs_sb[:, i * TPC + pr * 1024 + cs:
                                                i * TPC + pr * 1024 + cs + W],
                                         start=(i == 0),
                                         stop=(bias_row is None and i == NCH - 1))
                if bias_row is not None:
                    for h2 in range(2):
                        cs = h2 * W
                        last = TE.matmul(pb[:, cs:cs + W], bias_row[0:1, j * P:(j + 1) * P],
                                         onesw[0:1, cs:cs + W], start=False, stop=True)
                S.inc(pre + "_mm", last)
                consume(j, pr, pb)

    wf2_ready = load_weight(wf_in, wf2_sb)

    def o_consume(j, pr, pb):
        S.wait(VE, "o_mm")
        xs = x_st[:, j * TPC + pr * 1024:j * TPC + (pr + 1) * 1024]
        S.inc("o_ev", VE.scalar_tensor_tensor(xs, pb[:, :], bo_sb[:, j:j + 1], xs,
                                              ALU.add, ALU.add))

    def wait_ctx(eng, pr):
        eng.wait_ge(S.get("n_ev")[0], nrm_half[pr])

    def chase_nrm(eng, pr, i, cs):
        eng.wait_ge(S.get("n_ev")[0], 8 * pr + i + 1)

    def wait_wo(eng):
        eng.wait_ge(S.get("s_wc")[0], wo_ready)

    emit_proj4(w_a, ctx_sb, o_consume, "o", wait_ctx, wait_wo, pr_outer=True,
               chunk_wait=chase_nrm)
    o_ev_done = S.n("o_ev")
    o_mm_done = S.n("o_mm")

    def src_wait_x2(eng, s, c):
        eng.wait_ge(S.get("o_ev")[0], min(8 * (s // 2) + c + 1, o_ev_done))
        eng.wait_ge(S.get("o_mm")[0], min(8 * (s // 2) + c + 1, o_mm_done))

    ln3_done = []

    def wait_u3(eng, pr):
        eng.wait_ge(S.get("l3_u")[0], ln3_done[2 * pr + 1])

    def chase_l3(eng, pr, i, cs):
        eng.wait_ge(S.get("l3_u")[0], 8 * (2 * pr + cs // W) + i + 1)

    def wait_wf2(eng):
        eng.wait_ge(S.get("s_wc")[0], wf2_ready)

    def f2_consume(j, pr, pb):
        ffn_consume("f2")(j, pr * 1024, pb)
        S.wait(SY, "f2_ev")
        S.dma("s_st", SY.dma_start(
            out=y_out[j * P:(j + 1) * P, pr * 1024:(pr + 1) * 1024],
            in_=x_st[:, j * TPC + pr * 1024:j * TPC + (pr + 1) * 1024]))

    for half in range(2):
        for s in (2 * half, 2 * half + 1):
            ln3_done += emit_ln4(src_wait_x2, "l3", u2_bf, slabs=[s])
        emit_proj4(wf2_sb, u2_bf, f2_consume, "f2", wait_u3, wait_wf2,
                   bias_row=fbrow, chunk_wait=chase_l3, prs=[half])
    S.wait(SY, "s_st")

    stD.close()
    stL3.close()
    stL2a.close()
    stack.close()
    return nc


_NC_CACHE = {}


def kernel(L0_attn_x, L0_attn_Wq, L0_attn_Wk, L0_attn_Wv, L0_attn_Wo, L0_attn_bo,
           L0_ffn_weight_fp8, L0_ffn_bias, L0_ffn_input_scale, L0_ffn_weight_scale):
    x = np.ascontiguousarray(np.asarray(L0_attn_x, dtype=np.float32))
    B = x.shape[0]
    assert x.shape == (4, SEQ, D)

    if "nc" not in _NC_CACHE:
        _NC_CACHE["nc"] = build_nc()
    nc = _NC_CACHE["nc"]

    wq = np.ascontiguousarray(np.asarray(L0_attn_Wq, dtype=np.float32))
    wk = np.ascontiguousarray(np.asarray(L0_attn_Wk, dtype=np.float32))
    wv = np.ascontiguousarray(np.asarray(L0_attn_Wv, dtype=np.float32))
    wo = np.ascontiguousarray(np.asarray(L0_attn_Wo, dtype=np.float32))
    wf_t = np.ascontiguousarray(np.asarray(L0_ffn_weight_fp8, dtype=np.float32).T)
    bo_rs = np.ascontiguousarray(np.asarray(L0_attn_bo, dtype=np.float32).reshape(NCH, P).T)
    fbias = np.ascontiguousarray(np.asarray(L0_ffn_bias, dtype=np.float32).reshape(1, D))
    in_s = np.asarray(L0_ffn_input_scale, dtype=np.float32).reshape(1, 1)
    w_s = np.asarray(L0_ffn_weight_scale, dtype=np.float32).reshape(1, 1)

    tok_idx = {0: list(range(0, 1024)) + list(range(3072, 4096)),
               1: list(range(1024, 3072))}
    in_maps = []
    for core in range(8):
        b, parity = core // 2, core % 2
        xs = np.ascontiguousarray(x[b, tok_idx[parity], :].T)
        in_maps.append({
            "x_fm": xs, "wq": wq, "wk": wk, "wv": wv, "wo": wo, "wf_t": wf_t,
            "bo_rs": bo_rs, "ffn_bias": fbias, "in_scale": in_s, "w_scale": w_s,
            "parity": np.array([[parity]], dtype=np.int32),
        })

    res = run_bass_kernel_spmd(nc, in_maps, core_ids=list(range(8)))

    out = np.empty((B, SEQ, D), dtype=np.float32)
    for core in range(8):
        b, parity = core // 2, core % 2
        out[b, tok_idx[parity], :] = res.results[core]["y_fm"].T
    return out
